# version 15 (final)
# baseline (speedup 1.0000x reference)
import contextlib
from collections import defaultdict

import numpy as np
import ml_dtypes

import concourse.bass as bass
import concourse.mybir as mybir
from concourse.bass_utils import run_bass_kernel_spmd

F32 = mybir.dt.float32
BF16 = mybir.dt.bfloat16
I32 = mybir.dt.int32
AF = mybir.ActivationFunctionType
ALU = mybir.AluOpType
AX = mybir.AxisListType

NCORES = 8
SEQ = 2
L = 2048
DM = 2048
DH = 8192
NH = 16
HD = 128
T = 512
NT = L // T
NDC = DM // 128
NHC = DH // 128
SCALE = float(1.0 / np.sqrt(HD))
EPS = 1e-5

ENGINES = ("sync", "gpsimd", "tensor", "vector", "scalar")


class Plan:

    def __init__(self):
        self.q = {e: [] for e in ENGINES}
        self.cnt = defaultdict(int)
        self.S = {}

    def wait(self, eng, sem, val=None):
        v = self.cnt[sem] if val is None else val
        if v <= 0:
            return
        self.q[eng].append(lambda e, s=sem, vv=v: e.wait_ge(self.S[s], vv))

    def op(self, eng, fn, inc=None, n=1):
        if inc is None:
            self.q[eng].append(fn)
            return None
        self.cnt[inc] += n
        self.q[eng].append(
            lambda e, f=fn, s=inc, k=n: f(e).then_inc(self.S[s], k))
        return self.cnt[inc]

    def dma(self, eng, out, in_, sem, **kw):
        self.cnt[sem] += 16
        self.q[eng].append(
            lambda e, o=out, i=in_, s=sem, kww=dict(kw):
                e.dma_start(o, i, **kww).then_inc(self.S[s], 16))
        return self.cnt[sem]


class K:
    pass


def _psum_slot(k, bank):
    if bank < 4:
        return k.PB1[:, bank * 512:(bank + 1) * 512]
    return k.PB2[:, (bank - 4) * 512:(bank - 4 + 1) * 512]


def emit_p0(k):
    p = k.p
    p.op("vector", lambda e: e.memset(k.ones_r, 1.0), inc="dve")
    p.op("vector", lambda e: e.memset(k.ones_c[:], 1.0), inc="dve")
    p.op("vector", lambda e: e.memset(k.ones_f[:], 1.0), inc="dve")
    p.op("vector", lambda e: e.memset(k.tup_aug[:], 1.0), inc="dve")
    p.op("vector", lambda e: e.memset(k.tdn_aug[:], 0.5), inc="dve")

    p.dma("gpsimd", k.bup_aug[0:16, :], k.IN["L0_ffn_w_up_lora_b"].ap(), "st")
    p.dma("gpsimd", k.bup_aug[16:17, :], k.IN["L0_ffn_b_up"].ap()[None, :], "st")
    p.dma("gpsimd", k.bdn_aug[0:16, :], k.IN["L0_ffn_w_down_lora_b"].ap(), "st")
    p.dma("gpsimd", k.bdn_aug[16:17, :], k.IN["L0_ffn_b_down"].ap()[None, :], "st")
    p.dma("gpsimd", k.aup[:].rearrange("p (dc r) -> p dc r", r=16),
          k.IN["L0_ffn_w_up_lora_a"].ap().rearrange("(dc p) r -> p dc r", p=128), "st")
    p.dma("gpsimd", k.adn[:].rearrange("p (hc r) -> p hc r", r=16),
          k.IN["L0_ffn_w_down_lora_a"].ap().rearrange("(hc p) r -> p hc r", p=128), "st")
    p.dma("gpsimd", k.bqkv_r[:], k.IN["L0_attn_b_qkv"].ap()[None, :], "st")
    p.dma("gpsimd", k.bproj_r[:], k.IN["L0_attn_b_proj"].ap()[None, :], "st")
    k.smalls_done = p.cnt["st"]

    p.dma("sync", k.cstc[:], k.IN["cst"].ap(), "xl")
    upsc = k.xres[:, 0:2048]
    dnsc = k.xres[:, 2048:4096]
    p.dma("sync", upsc.rearrange("p (hc c) -> p hc c", c=32),
          k.IN["L0_ffn_w_up_scale"].ap().rearrange("(hc p) c -> p hc c", p=128), "xl")
    p.dma("sync", dnsc.rearrange("p (dc c) -> p dc c", c=128),
          k.IN["L0_ffn_w_down_scale"].ap().rearrange("(dc p) c -> p dc c", p=128), "xl")
    sc_mark = p.cnt["xl"]
    p.wait("vector", "xl", sc_mark)
    p.op("vector", lambda e: e.tensor_scalar(dnsc, dnsc, 0.5, None, ALU.mult), inc="dve")
    dns_mark = p.cnt["dve"]

    wdq = k.IN["L0_ffn_w_down_q"].ap()
    wuq = k.IN["L0_ffn_w_up_q"].ap()
    dn_in = [k.wsl[0][:].bitcast(I32)[:, 0:2048], k.wsl[0][:].bitcast(I32)[:, 2048:4096],
             k.wsl[1][:].bitcast(I32)[:, 0:2048], k.wsl[1][:].bitcast(I32)[:, 2048:4096]]
    dn_sem = ["w0", "w1", "at2", "at3"]
    dn_out = [k.x2[:, i * 2048:(i + 1) * 2048] for i in range(4)]
    dn_osem = ["a0", "a1", "a2", "a3"]
    up_in = [k.xbf[:].bitcast(I32)[:, 0:2048], k.xbf[:].bitcast(I32)[:, 2048:4096],
             k.x2[:, 20480:28672].bitcast(I32)[:, 0:2048],
             k.x2[:, 20480:28672].bitcast(I32)[:, 2048:4096]]
    up_sem = ["u0", "u1", "xg0", "xg1"]
    up_out = [k.x2[:, 16384 + i * 2048: 16384 + (i + 1) * 2048] for i in range(2)]
    up_osem = ["qk0", "qk1"]
    dn_ld = [0] * 4
    dn_act = [0] * 4
    dn_st = [0] * 4
    up_ld = [0] * 4
    up_act = [0] * 4
    up_st = [0] * 2

    def dn_load(ch):
        c, m = ch // 4, ch % 4
        sl = ch % 4
        if ch >= 4:
            p.wait("sync", "act", dn_act[sl])
        dn_ld[sl] = p.dma("sync", dn_in[sl],
                          wdq[c * 128:(c + 1) * 128, m * 2048:(m + 1) * 2048],
                          dn_sem[sl])

    def dn_comp(ch):
        c, m = ch // 4, ch % 4
        sl = ch % 4
        p.wait("scalar", dn_sem[sl], dn_ld[sl])
        if ch >= 4:
            p.wait("scalar", dn_osem[sl], dn_st[sl])
        p.op("scalar", lambda e, o=dn_out[sl], i=dn_in[sl]:
             e.activation(o, i, AF.Copy, bias=-7.5, scale=1.0), inc="act")
        dn_act[sl] = p.cnt["act"]
        if ch == 0:
            p.wait("vector", "dve", dns_mark)
        p.wait("vector", "act", dn_act[sl])
        sc = dnsc.rearrange("p (dc c2) -> p dc c2", c2=128)[:, c, m * 32:(m + 1) * 32]
        p.op("vector", lambda e, o=dn_out[sl].rearrange("p (a b) -> p a b", b=64),
             s=sc[:, :, None].broadcast_to([128, 32, 64]):
             e.tensor_tensor(o, o, s, ALU.mult), inc="dve")
        p.wait("gpsimd", "dve")
        dn_st[sl] = p.dma("gpsimd",
                          k.WdnN.ap()[c * 128:(c + 1) * 128, m * 2048:(m + 1) * 2048],
                          dn_out[sl], dn_osem[sl])

    def up_load(c):
        sl = c % 4
        if c >= 4:
            p.wait("scalar", "act", up_act[sl])
        p.op("scalar", lambda e, i=up_in[sl], c=c:
             e.dma_start(i, wuq[c * 128:(c + 1) * 128, :]), inc=up_sem[sl], n=16)
        up_ld[sl] = p.cnt[up_sem[sl]]

    def up_comp(c):
        sl = c % 4
        osl = c % 2
        p.wait("scalar", up_sem[sl], up_ld[sl])
        if c >= 2:
            p.wait("scalar", up_osem[osl], up_st[osl])
        p.op("scalar", lambda e, o=up_out[osl], i=up_in[sl]:
             e.activation(o, i, AF.Copy, bias=-7.5, scale=1.0), inc="act")
        up_act[sl] = p.cnt["act"]
        p.wait("vector", "act", up_act[sl])
        sc = upsc.rearrange("p (hc c2) -> p hc c2", c2=32)[:, c, :]
        p.op("vector", lambda e, o=up_out[osl].rearrange("p (a b) -> p a b", b=64),
             s=sc[:, :, None].broadcast_to([128, 32, 64]):
             e.tensor_tensor(o, o, s, ALU.mult), inc="dve")
        p.wait("gpsimd", "dve")
        up_st[osl] = p.dma("gpsimd", k.WupN.ap()[c * 128:(c + 1) * 128, :],
                           up_out[osl], up_osem[osl])

    for step in range(64):
        dn_load(step)
        up_load(step)
        if step >= 3:
            dn_comp(step - 3)
            up_comp(step - 3)
    for step in range(61, 64):
        dn_comp(step)
        up_comp(step)
    dn_nat_done = (p.cnt["a0"], p.cnt["a1"], p.cnt["a2"], p.cnt["a3"])
    up_nat_done = (p.cnt["qk0"], p.cnt["qk1"])

    for i in range(4):
        p.wait("sync", "a%d" % i, dn_nat_done[i])
    ld_mark = [0, 0]
    st2 = [0, 0]
    for hc in range(64):
        par = hc % 2
        if hc >= 2:
            p.wait("sync", "qs%d" % par, st2[par])
        ld_mark[par] = p.dma("sync", k.abuf[par],
                             k.WdnN.ap()[0:2048, hc * 128:(hc + 1) * 128],
                             "at%d" % par, transpose=True)
        p.wait("gpsimd", "at%d" % par, ld_mark[par])
        st2[par] = p.dma("gpsimd", k.WdnT.ap()[hc * 128:(hc + 1) * 128, :],
                         k.abuf[par], "qs%d" % par)

    p.wait("sync", "qk0", up_nat_done[0])
    p.wait("sync", "qk1", up_nat_done[1])
    p.wait("sync", "qs0")
    p.wait("sync", "qs1")
    slots = (k.x2[:, 0:8192], k.x2[:, 8192:16384])
    ld_mark = [0, 0]
    st2 = [0, 0]
    for dc in range(16):
        par = dc % 2
        if dc >= 2:
            p.wait("sync", "qs%d" % par, st2[par])
        ld_mark[par] = p.dma("sync", slots[par],
                             k.WupN.ap()[0:8192, dc * 128:(dc + 1) * 128],
                             "at%d" % par, transpose=True)
        p.wait("gpsimd", "at%d" % par, ld_mark[par])
        st2[par] = p.dma("gpsimd", k.WupT.ap()[dc * 128:(dc + 1) * 128, :],
                         slots[par], "qs%d" % par)

    p.dma("gpsimd", k.WqkvB.ap(), k.IN["L0_attn_w_qkv"].ap(), "u2")
    p.dma("gpsimd", k.WprojB.ap(), k.IN["L0_attn_w_proj"].ap(), "u2")
    k.cast_mark = p.cnt["u2"]

    k.p0_done = dict(qs0=p.cnt["qs0"], qs1=p.cnt["qs1"], smalls=k.smalls_done)


def emit_ln(k, pe_guard=None, xl_marks=None):
    p = k.p
    cast_marks = []
    for c in range(NDC):
        if xl_marks is not None and c % 4 == 0:
            p.wait("vector", xl_marks[c // 4][0], xl_marks[c // 4][1])
        p.op("vector", lambda e, c=c: e.tensor_copy(
            k.xbf[:, c * T:(c + 1) * T], k.xres[:, c * T:(c + 1) * T]), inc="dve")
        cast_marks.append(p.cnt["dve"])
    cast_mark = p.cnt["dve"]
    LNm = k.PB1[0:1, 0:512]
    LNs2 = k.PB1[0:1, 512:1024]
    if pe_guard:
        for s, v in pe_guard.items():
            p.wait("tensor", s, v)
    for c in range(NDC):
        p.wait("tensor", "dve", cast_marks[c])
        p.op("tensor", lambda e, c=c: e.matmul(
            LNm, k.ones_c[:], k.xbf[:, c * T:(c + 1) * T],
            start=(c == 0), stop=(c == NDC - 1)),
            inc=("pe" if c == NDC - 1 else None))
    for c in range(NDC):
        p.wait("scalar", "dve", cast_marks[c])
        if c >= 2:
            p.wait("scalar", "pe")
        p.op("scalar", lambda e, c=c: e.activation(
            k.sqb[:, (c % 2) * 512:(c % 2 + 1) * 512],
            k.xbf[:, c * T:(c + 1) * T], AF.Square), inc="act")
        p.wait("tensor", "act")
        p.op("tensor", lambda e, c=c: e.matmul(
            LNs2, k.ones_c[:], k.sqb[:, (c % 2) * 512:(c % 2 + 1) * 512],
            start=(c == 0), stop=(c == NDC - 1)), inc="pe")
    stats_mark = p.cnt["pe"]
    p.wait("vector", "pe", stats_mark)
    p.op("vector", lambda e: e.tensor_scalar(k.mrow, LNm, 1.0 / DM, None, ALU.mult), inc="dve")
    p.op("vector", lambda e: e.tensor_scalar(k.vrow, LNs2, 1.0 / DM, None, ALU.mult), inc="dve")
    p.op("vector", lambda e: e.tensor_tensor(k.sbr, k.mrow, k.mrow, ALU.mult), inc="dve")
    p.op("vector", lambda e: e.tensor_tensor(k.vrow, k.vrow, k.sbr, ALU.subtract), inc="dve")
    p.wait("scalar", "dve")
    p.op("scalar", lambda e: e.activation(k.rstd, k.vrow, AF.Sqrt, bias=k.cstc[0:1, 0:1]), inc="act")
    p.wait("vector", "act")
    p.op("vector", lambda e: e.reciprocal(k.vrow, k.rstd), inc="dve")
    p.wait("vector", "dve")
    p.op("vector", lambda e: e.scalar_tensor_tensor(
        k.sbr, k.mrow, -1.0, k.vrow, ALU.mult, ALU.mult), inc="dve")
    row_mark = p.cnt["dve"]
    SAB = k.PB1[:, 1024:1536]
    SBB = k.PB1[:, 1536:2048]
    p.wait("tensor", "dve", row_mark)
    p.op("tensor", lambda e: e.matmul(SAB, k.ones_f[:], k.vrow, start=True, stop=True), inc=None)
    p.op("tensor", lambda e: e.matmul(SBB, k.ones_f[:], k.sbr, start=True, stop=True), inc="pe")
    bc_mark = p.cnt["pe"]
    p.wait("vector", "pe", bc_mark)
    for c in range(NDC):
        p.op("vector", lambda e, c=c: e.tensor_tensor(
            k.xbf[:, c * T:(c + 1) * T], k.xres[:, c * T:(c + 1) * T], SAB, ALU.mult), inc="dve")
        p.op("vector", lambda e, c=c: e.tensor_tensor(
            k.xbf[:, c * T:(c + 1) * T], k.xbf[:, c * T:(c + 1) * T], SBB, ALU.add), inc="dve")
    k.xn_mark = p.cnt["dve"]


def emit_ffn(k):
    p = k.p
    TUP = k.PB2[0:16, 0:512]
    p.wait("tensor", "dve", k.xn_mark)
    for dc in range(NDC):
        p.op("tensor", lambda e, dc=dc: e.matmul(
            TUP, k.aup[:, dc * 16:(dc + 1) * 16], k.xbf[:, dc * T:(dc + 1) * T],
            start=(dc == 0), stop=(dc == NDC - 1)),
            inc=("pe" if dc == NDC - 1 else None))
    p.wait("vector", "pe")
    p.op("vector", lambda e: e.tensor_copy(k.tup_aug[0:16, :], TUP), inc="dve")
    tup_mark = p.cnt["dve"]

    evac_mark = [0] * 8
    wsl_free = list(k.wsl_free)
    wview = [w[:].rearrange("p (dc h) -> p dc h", h=512) for w in k.wsl]
    for u in range(16):
        par = u % 2
        p.wait("sync", "pe", wsl_free[par])
        wm = p.dma("sync", wview[par],
                   k.WupT.ap().rearrange("(dc p) h -> p dc h", p=128)
                   [:, :, u * 512:(u + 1) * 512], "w%d" % par)
        for hcl in range(4):
            i = u * 4 + hcl
            bank = i % 8
            p.wait("tensor", "w%d" % par, wm)
            if i == 0:
                p.wait("tensor", "dve", tup_mark)
            if i >= 8:
                p.wait("tensor", "act", evac_mark[bank])
            slot = _psum_slot(k, bank)
            for dc in range(NDC):
                p.op("tensor", lambda e, hcl=hcl, dc=dc, slot=slot, par=par:
                     e.matmul(slot, k.wsl[par][:, dc * 512 + hcl * 128: dc * 512 + (hcl + 1) * 128],
                              k.xbf[:, dc * T:(dc + 1) * T],
                              start=(dc == 0), stop=False), inc=None)
            p.op("tensor", lambda e, i=i, slot=slot:
                 e.matmul(slot, k.bup_aug[0:17, i * 128:(i + 1) * 128],
                          k.tup_aug[0:17, :], start=False, stop=True), inc="pe")
            pe_m = p.cnt["pe"]
            p.wait("scalar", "pe", pe_m)
            if i == 0:
                p.wait("scalar", "pe", k.x2_free)
                for gs, gv in k.x2_guards:
                    p.wait("scalar", gs, gv)
                k.x2_guards = []
            p.op("scalar", lambda e, i=i, slot=slot: e.activation(
                k.x2[:, i * T:(i + 1) * T], slot, AF.Relu), inc="act")
            evac_mark[bank] = p.cnt["act"]
        wsl_free[par] = p.cnt["pe"]

    x2_done = p.cnt["act"]
    TDN = k.PB2[0:16, 0:512]
    p.wait("tensor", "act", x2_done)
    for hc in range(NHC):
        p.op("tensor", lambda e, hc=hc: e.matmul(
            TDN, k.adn[:, hc * 16:(hc + 1) * 16], k.x2[:, hc * T:(hc + 1) * T],
            start=(hc == 0), stop=(hc == NHC - 1)),
            inc=("pe" if hc == NHC - 1 else None))
    p.wait("scalar", "pe")
    p.op("scalar", lambda e: e.activation(k.tdn_aug[0:16, :], TDN, AF.Copy, scale=0.5), inc="act")
    tdn_mark = p.cnt["act"]

    dvac_mark = [0] * 8
    wview2 = [w[:].rearrange("p (hc d) -> p hc d", d=128) for w in k.wsl]
    for sl in range(16):
        oc = sl
        bank = oc % 8
        par = sl % 2
        p.wait("sync", "pe", wsl_free[par])
        wm = p.dma("sync", wview2[par],
                   k.WdnT.ap().rearrange("(hc p) d -> p hc d", p=128)
                   [:, :, oc * 128:(oc + 1) * 128], "w%d" % par)
        p.wait("tensor", "w%d" % par, wm)
        if sl == 0:
            p.wait("tensor", "act", tdn_mark)
        if oc >= 8:
            p.wait("tensor", "dve", dvac_mark[bank])
        slot = _psum_slot(k, bank)
        for hc in range(NHC):
            p.op("tensor", lambda e, hc=hc, slot=slot, par=par:
                 e.matmul(slot, k.wsl[par][:, hc * 128:(hc + 1) * 128],
                          k.x2[:, hc * T:(hc + 1) * T],
                          start=(hc == 0), stop=False), inc=None)
        p.op("tensor", lambda e, oc=oc, slot=slot:
             e.matmul(slot, k.bdn_aug[0:17, oc * 128:(oc + 1) * 128],
                      k.tdn_aug[0:17, :], start=False, stop=True), inc="pe")
        pe_m = p.cnt["pe"]
        p.wait("vector", "pe", pe_m)
        p.op("vector", lambda e, oc=oc, slot=slot: e.tensor_tensor(
            k.xres[:, oc * T:(oc + 1) * T], slot,
            k.xres[:, oc * T:(oc + 1) * T], ALU.add), inc="dve")
        dvac_mark[bank] = p.cnt["dve"]
        wsl_free[par] = p.cnt["pe"]
    k.wsl_free = wsl_free
    k.x2_free = p.cnt["pe"]
    k.ffn_done = p.cnt["dve"]


def emit_p1_tile(k, s, t):
    p = k.p
    xv = k.xres[:].rearrange("p (dc t) -> p dc t", t=T)
    p.wait("sync", "xs", k.xres_store_mark)
    src_ap = k.IN["x"].ap()[s].rearrange("(dc p) t -> p dc t", p=128)
    xl_marks = []
    for g in range(4):
        xl_marks.append(("xg%d" % g,
                         p.dma("sync", xv[:, g * 4:(g + 1) * 4, :],
                               src_ap[:, g * 4:(g + 1) * 4, t * T:(t + 1) * T],
                               "xg%d" % g)))
    p.wait("vector", "pe", k.xbf_free)
    emit_ln(k, pe_guard={"act": k.ln_bank_guard_act, "dve": k.ln_bank_guard_dve},
            xl_marks=xl_marks)
    emit_ffn(k)
    k.xbf_free = p.cnt["pe"]
    k.ln_bank_guard_act = p.cnt["act"]
    k.ln_bank_guard_dve = p.cnt["dve"]
    p.wait("gpsimd", "dve", k.ffn_done)
    dst = (k.OUT.ap() if k.stage in ("p1", "p1dbg") else k.X1.ap())
    k.xres_store_mark = p.dma(
        "gpsimd", dst[s].rearrange("(dc p) t -> p dc t", p=128)[:, :, t * T:(t + 1) * T],
        xv, "xs")


def emit_p1b_tile(k, s, t):
    p = k.p
    xv = k.xres[:].rearrange("p (dc t) -> p dc t", t=T)
    p.wait("sync", "xs", k.xres_store_mark)
    src_ap = k.X1.ap()[s].rearrange("(dc p) t -> p dc t", p=128)
    xl_marks = []
    for g in range(4):
        xl_marks.append(("xg%d" % g,
                         p.dma("sync", xv[:, g * 4:(g + 1) * 4, :],
                               src_ap[:, g * 4:(g + 1) * 4, t * T:(t + 1) * T],
                               "xg%d" % g)))
    p.wait("vector", "pe", k.xbf_free)
    emit_ln(k, pe_guard={"act": k.ln_bank_guard_act, "dve": k.ln_bank_guard_dve},
            xl_marks=xl_marks)

    wsl_free = list(k.wsl_free)
    qs_mark = [0, 0]
    targets = (k.QS, k.QS, k.KS, k.KS, k.VS, k.VS)
    wview = [w[:].rearrange("p (dc o) -> p dc o", o=512) for w in k.wsl]
    evac_mark = [0] * 8
    first_evac = True
    for sl in range(12):
        par = sl % 2
        wave = sl // 2
        spar = wave % 2
        p.wait("sync", "pe", wsl_free[par])
        if sl < 2:
            p.wait("sync", "u2", k.cast_mark)
        wm = p.dma("sync", wview[par],
                   k.WqkvB.ap().rearrange("(dc p) o -> p dc o", p=128)
                   [:, :, sl * 512:(sl + 1) * 512], "w%d" % par)
        for ocl in range(4):
            oc = sl * 4 + ocl
            bank = oc % 8
            p.wait("tensor", "w%d" % par, wm)
            if oc == 0:
                p.wait("tensor", "dve", k.xn_mark)
            if oc >= 8:
                p.wait("tensor", "act", evac_mark[bank])
            slot = _psum_slot(k, bank)
            for dc in range(NDC):
                p.op("tensor", lambda e, dc=dc, ocl=ocl, slot=slot, par=par:
                     e.matmul(slot, k.wsl[par][:, dc * 512 + ocl * 128: dc * 512 + (ocl + 1) * 128],
                              k.xbf[:, dc * T:(dc + 1) * T],
                              start=(dc == 0), stop=False), inc=None)
            p.op("tensor", lambda e, oc=oc, slot=slot:
                 e.matmul(slot, k.bqkv_r[0:1, oc * 128:(oc + 1) * 128],
                          k.ones_r[0:1, 0:T], start=False, stop=True), inc="pe")
            pe_m = p.cnt["pe"]
            p.wait("scalar", "pe", pe_m)
            if first_evac:
                p.wait("scalar", "pe", k.x2_free)
                for gs, gv in k.x2_guards:
                    p.wait("scalar", gs, gv)
                k.x2_guards = []
                first_evac = False
            if sl % 2 == 0 and ocl == 0:
                p.wait("scalar", "qs%d" % spar, qs_mark[spar])
            sc = k.cstc[:, 1:2] if oc < 16 else 1.0
            pos = (sl % 2) * 4 + ocl
            p.op("scalar", lambda e, pos=pos, slot=slot, spar=spar, sc=sc:
                 e.activation(k.qst[spar][:, pos * T:(pos + 1) * T], slot,
                              AF.Copy, scale=sc), inc="act")
            evac_mark[bank] = p.cnt["act"]
        wsl_free[par] = p.cnt["pe"]
        if sl % 2 == 1:
            p.wait("gpsimd", "act", p.cnt["act"])
            tgt = targets[wave]
            half = wave % 2
            qs_mark[spar] = p.dma(
                "gpsimd",
                tgt.ap()[s].rearrange("(oc p) t -> p oc t", p=128)
                [:, half * 8:(half + 1) * 8, t * T:(t + 1) * T],
                k.qst[spar].rearrange("p (o t) -> p o t", t=T), "qs%d" % spar)
        k.wsl_free = wsl_free
    k.xbf_free = p.cnt["pe"]
    k.ln_bank_guard_act = p.cnt["act"]
    k.ln_bank_guard_dve = p.cnt["dve"]
    k.qkv_spill_marks = (p.cnt["qs0"], p.cnt["qs1"])
    k.xres_store_mark = p.cnt["xs"]


def emit_vtm_bounce(k, s):
    p = k.p
    p.wait("sync", "pe", k.x2_free)
    for gs, gv in k.x2_guards:
        p.wait("sync", gs, gv)
    p.wait("sync", "qs0", k.qkv_spill_marks[0])
    p.wait("sync", "qs1", k.qkv_spill_marks[1])
    ld = [0, 0]
    st = [k.vtb_st0, k.vtb_st1]
    for tc in range(16):
        par = tc % 2
        if tc >= 2:
            p.wait("sync", "qs%d" % par, st[par])
        ld[par] = p.dma("sync", k.atb[par],
                        k.VS.ap()[s][0:2048, tc * 128:(tc + 1) * 128],
                        "at%d" % par, transpose=True)
        p.wait("gpsimd", "at%d" % par, ld[par])
        st[par] = p.dma("gpsimd", k.VTM.ap()[tc * 128:(tc + 1) * 128, :],
                        k.atb[par], "qs%d" % par)
    k.vtb_st0, k.vtb_st1 = st
    k.vtb_done = (p.cnt["qs0"], p.cnt["qs1"])
    k.prev_at = [p.cnt["at%d" % i] for i in range(4)]


def emit_p2a_head(k, s, h):
    p = k.p
    par = h % 2
    p.wait("sync", "pe", k.head_consumed[par])
    if h == 0:
        p.wait("sync", "qs0", k.vtb_done[0])
        p.wait("sync", "qs1", k.vtb_done[1])
    p.dma("sync", k.qh[par], k.QS.ap()[s][h * 128:(h + 1) * 128, :], "qk%d" % par)
    p.dma("sync", k.kh[par], k.KS.ap()[s][h * 128:(h + 1) * 128, :], "qk%d" % par)
    p.dma("sync", k.vtm[par][:].rearrange("p (kc j) -> p kc j", j=128),
          k.VTM.ap().rearrange("(kc p) f -> p kc f", p=128)[:, :, h * 128:(h + 1) * 128],
          "qk%d" % par)
    ld_mark = p.cnt["qk%d" % par]

    exp_mark = [0, 0]
    a_store = list(k.a_store)
    for i in range(4):
        p.wait("gpsimd", "at%d" % i, k.prev_at[i])

    def emit_norm(qc):
        qb = qc % 4
        p.op("vector", lambda e, qb=qb: e.reciprocal(k.linv[qb][:], k.lcol[qb][:]), inc="dve")
        p.wait("vector", "dve")
        p.op("vector", lambda e, qb=qb: e.tensor_scalar(
            k.abuf[qb], k.abuf[qb], k.linv[qb][:], None, ALU.mult), inc="dve")
        p.wait("gpsimd", "dve")
        a_store[qb] = p.dma("gpsimd", k.AS.ap()[qc * 128:(qc + 1) * 128, :],
                            k.abuf[qb], "a%d" % qb)

    for qc in range(16):
        qpar = qc % 2
        qb = qc % 4
        PB = k.PB1 if qpar == 0 else k.PB2
        if qc == 0:
            p.wait("tensor", "qk%d" % par, ld_mark)
        if qc >= 2:
            p.wait("tensor", "act", exp_mark[qpar])
        elif qc == 0:
            p.wait("tensor", "act", k.prev_oevac)
        else:
            p.wait("tensor", "act", k.prev_exp15)
        for kr in range(4):
            p.op("tensor", lambda e, qc=qc, kr=kr, PB=PB, par=par:
                 e.matmul(PB[:, kr * 512:(kr + 1) * 512],
                          k.qh[par][:, qc * 128:(qc + 1) * 128],
                          k.kh[par][:, kr * 512:(kr + 1) * 512],
                          start=True, stop=True),
                 inc=("pe" if kr == 3 else None))
        p.wait("vector", "pe")
        p.op("vector", lambda e, qb=qb, PB=PB:
             e.reduce_max(k.nmax[qb][:], PB[:, :], AX.X, negate=True), inc="dve")
        max_mark = p.cnt["dve"]
        p.wait("scalar", "dve", max_mark)
        p.wait("scalar", "a%d" % qb, a_store[qb])
        p.op("scalar", lambda e, qb=qb, PB=PB: e.activation(
            k.abuf[qb], PB[:, :], AF.Exp, bias=k.nmax[qb][:],
            accum_out=k.lcol[qb][:]), inc="act")
        exp_mark[qpar] = p.cnt["act"]
        if qc == 15:
            k.prev_exp15 = exp_mark[qpar]
        if qc > 0:
            p.wait("vector", "act", exp_mark[(qc - 1) % 2])
            emit_norm(qc - 1)
    p.wait("vector", "act", exp_mark[15 % 2])
    emit_norm(15)
    k.a_store = a_store

    at_mark = [0, 0, 0, 0]
    av_pe = [0, 0, 0, 0]
    exp_all = p.cnt["act"]
    at_q = ("sync", "scalar")
    for kc in range(16):
        slot = kc % 4
        q = at_q[kc % 2]
        if kc >= 4:
            p.wait(q, "pe", av_pe[slot])
        else:
            for i in range(4):
                p.wait(q, "a%d" % i, a_store[i])
            if h == 0:
                p.wait(q, "qs0", k.vtb_done[0])
                p.wait(q, "qs1", k.vtb_done[1])
        at_mark[slot] = p.dma(q, k.atb[slot],
                              k.AS.ap()[0:2048, kc * 128:(kc + 1) * 128],
                              "at%d" % slot, transpose=True)
        p.wait("tensor", "at%d" % slot, at_mark[slot])
        if kc == 0:
            p.wait("tensor", "act", exp_all)
        for qr in range(4):
            p.op("tensor", lambda e, kc=kc, qr=qr, par=par, slot=slot:
                 e.matmul(k.PB1[:, qr * 512:(qr + 1) * 512],
                          k.vtm[par][:, kc * 128:(kc + 1) * 128],
                          k.atb[slot][:, qr * 512:(qr + 1) * 512],
                          start=(kc == 0), stop=(kc == 15),
                          skip_group_check=True),
                 inc=("pe" if qr == 3 else None))
        av_pe[slot] = p.cnt["pe"]
    k.head_consumed[par] = p.cnt["pe"]
    k.prev_at = [p.cnt["at%d" % i] for i in range(4)]
    p.wait("scalar", "pe")
    p.wait("scalar", "oh", k.oh_store)
    p.op("scalar", lambda e: e.activation(k.ohst, k.PB1[:, :], AF.Copy), inc="act")
    k.prev_oevac = p.cnt["act"]
    p.wait("gpsimd", "act", k.prev_oevac)
    k.oh_store = p.dma("gpsimd", k.OS.ap()[s][h * 128:(h + 1) * 128, :], k.ohst, "oh")


def emit_p2b_tile(k, s, t):
    p = k.p
    xv = k.xres[:].rearrange("p (dc t) -> p dc t", t=T)
    p.wait("sync", "xs", k.xres_store_mark)
    p.wait("sync", "dve", k.ln_bank_guard_dve)
    xm = p.dma("sync", xv,
               k.X1.ap()[s].rearrange("(dc p) t -> p dc t", p=128)
               [:, :, t * T:(t + 1) * T], "xl")
    p.wait("sync", "oh", k.oh_store)
    p.wait("sync", "pe", k.otile_free)
    om = p.dma("sync", k.otile.rearrange("p (fc t) -> p fc t", t=T),
               k.OS.ap()[s].rearrange("(fc p) t -> p fc t", p=128)
               [:, :, t * T:(t + 1) * T], "xl")

    wsl_free = list(k.wsl_free)
    proj_dvac = [0] * 8
    wview = [w[:].rearrange("p (dc o) -> p dc o", o=512) for w in k.wsl]
    for sl in range(4):
        par = sl % 2
        p.wait("sync", "pe", wsl_free[par])
        if sl < 2:
            p.wait("sync", "u2", k.cast_mark)
        wm = p.dma("sync", wview[par],
                   k.WprojB.ap().rearrange("(dc p) o -> p dc o", p=128)
                   [:, :, sl * 512:(sl + 1) * 512], "w%d" % par)
        for ocl in range(4):
            oc = sl * 4 + ocl
            bank = oc % 8
            p.wait("tensor", "w%d" % par, wm)
            if oc == 0:
                p.wait("tensor", "xl", om)
                p.wait("tensor", "act", k.prev_oevac)
                p.wait("tensor", "dve", k.ln_bank_guard_dve)
            if oc >= 8:
                p.wait("tensor", "dve", proj_dvac[bank])
            slot = _psum_slot(k, bank)
            for dc in range(NDC):
                p.op("tensor", lambda e, dc=dc, ocl=ocl, slot=slot, par=par:
                     e.matmul(slot, k.wsl[par][:, dc * 512 + ocl * 128: dc * 512 + (ocl + 1) * 128],
                              k.otile[:, dc * T:(dc + 1) * T],
                              start=(dc == 0), stop=False), inc=None)
            p.op("tensor", lambda e, oc=oc, slot=slot:
                 e.matmul(slot, k.bproj_r[0:1, oc * 128:(oc + 1) * 128],
                          k.ones_r[0:1, 0:T], start=False, stop=True), inc="pe")
            pe_m = p.cnt["pe"]
            p.wait("vector", "pe", pe_m)
            if oc == 0:
                p.wait("vector", "xl", om)
            p.op("vector", lambda e, oc=oc, slot=slot: e.tensor_tensor(
                k.xres[:, oc * T:(oc + 1) * T], slot,
                k.xres[:, oc * T:(oc + 1) * T], ALU.add), inc="dve")
            proj_dvac[bank] = p.cnt["dve"]
        wsl_free[par] = p.cnt["pe"]
    k.wsl_free = wsl_free
    k.otile_free = p.cnt["pe"]
    proj_done = p.cnt["dve"]

    if k.stage == "p2a":
        p.wait("gpsimd", "dve", proj_done)
        k.xres_store_mark = p.dma(
            "gpsimd",
            k.OUT.ap()[s].rearrange("(dc p) t -> p dc t", p=128)[:, :, t * T:(t + 1) * T],
            xv, "xs")
        return

    p.wait("vector", "pe", k.xbf_free)
    emit_ln(k, pe_guard={"act": k.ln_bank_guard_act, "dve": k.ln_bank_guard_dve})
    emit_ffn(k)
    k.xbf_free = p.cnt["pe"]
    k.ln_bank_guard_act = p.cnt["act"]
    k.ln_bank_guard_dve = p.cnt["dve"]
    p.wait("gpsimd", "dve", k.ffn_done)
    k.xres_store_mark = p.dma(
        "gpsimd",
        k.OUT.ap()[s].rearrange("(dc p) t -> p dc t", p=128)[:, :, t * T:(t + 1) * T],
        xv, "xs")


def build(stage="full"):
    nc = bass.Bass("TRN2", target_bir_lowering=False)
    k = K()
    k.stage = stage
    k.p = Plan()
    p = k.p

    k.IN = {}
    k.IN["x"] = nc.dram_tensor("x", [SEQ, DM, L], F32, kind="ExternalInput")
    k.IN["cst"] = nc.dram_tensor("cst", [128, 2], F32, kind="ExternalInput")
    for name, shape, dt in [
        ("L0_attn_w_qkv", [DM, 3 * DM], F32), ("L0_attn_b_qkv", [3 * DM], F32),
        ("L0_attn_w_proj", [DM, DM], F32), ("L0_attn_b_proj", [DM], F32),
        ("L0_ffn_w_up_q", [DH, DM], I32), ("L0_ffn_w_up_scale", [DH, DM // 64], F32),
        ("L0_ffn_b_up", [DH], F32), ("L0_ffn_w_up_lora_a", [DM, 16], F32),
        ("L0_ffn_w_up_lora_b", [16, DH], F32), ("L0_ffn_w_down_q", [DM, DH], I32),
        ("L0_ffn_w_down_scale", [DM, DH // 64], F32), ("L0_ffn_b_down", [DM], F32),
        ("L0_ffn_w_down_lora_a", [DH, 16], F32), ("L0_ffn_w_down_lora_b", [16, DM], F32),
    ]:
        k.IN[name] = nc.dram_tensor(name, shape, dt, kind="ExternalInput")
    k.OUT = nc.dram_tensor("out", [SEQ, DM, L], F32, kind="ExternalOutput")
    if stage == "p2a":
        k.DOS = nc.dram_tensor("dos", [SEQ, DM, L], BF16, kind="ExternalOutput")
        k.DQS = nc.dram_tensor("dqs", [SEQ, DM, L], BF16, kind="ExternalOutput")
        k.DKS = nc.dram_tensor("dks", [SEQ, DM, L], BF16, kind="ExternalOutput")
        k.DVS = nc.dram_tensor("dvs", [SEQ, DM, L], BF16, kind="ExternalOutput")
        k.DAS = nc.dram_tensor("das", [L, L], BF16, kind="ExternalOutput")
    if stage == "p1dbg":
        k.DXN = nc.dram_tensor("dxn", [128, 8192], BF16, kind="ExternalOutput")
        k.DX2 = nc.dram_tensor("dx2", [128, 32768], BF16, kind="ExternalOutput")
        k.DW1 = nc.dram_tensor("dw1", [128, 8192], BF16, kind="ExternalOutput")
        k.DWT = nc.dram_tensor("dwt", [DH, 256], BF16, kind="ExternalOutput")

    k.WqkvB = nc.dram_tensor("WqkvB", [DM, 3 * DM], BF16)
    k.WprojB = nc.dram_tensor("WprojB", [DM, DM], BF16)
    k.WupN = nc.dram_tensor("WupN", [DH, DM], BF16)
    k.WupT = nc.dram_tensor("WupT", [DM, DH], BF16)
    k.WdnN = nc.dram_tensor("WdnN", [DM, DH], BF16)
    k.WdnT = nc.dram_tensor("WdnT", [DH, DM], BF16)
    k.X1 = nc.dram_tensor("X1", [SEQ, DM, L], F32)
    k.QS = nc.dram_tensor("QS", [SEQ, DM, L], BF16)
    k.KS = nc.dram_tensor("KS", [SEQ, DM, L], BF16)
    k.VS = nc.dram_tensor("VS", [SEQ, DM, L], BF16)
    k.AS = nc.dram_tensor("AS", [L, L], BF16)
    k.VTM = nc.dram_tensor("VTM", [L, DM], BF16)
    k.OS = nc.dram_tensor("OS", [SEQ, DM, L], BF16)

    ctx = contextlib.ExitStack()
    sb = lambda name, shape, dt: ctx.enter_context(nc.sbuf_tensor(name, shape, dt))
    with ctx:
        k.wsl = [sb("wsl0", [128, 8192], BF16), sb("wsl1", [128, 8192], BF16)]
        k.xres = sb("xres", [128, 8192], F32)
        k.xbf = sb("xbf", [128, 8192], BF16)
        k.sqb = sb("sqb", [128, 1024], BF16)
        k.x2 = sb("x2", [128, 32768], BF16)
        k.otile = k.x2[:, 24576:32768]
        k.qst = [k.x2[:, 0:4096], k.x2[:, 4096:8192]]
        k.abuf = [k.x2[:, 0:2048], k.x2[:, 2048:4096],
                  k.x2[:, 4096:6144], k.x2[:, 6144:8192]]
        k.atb = [k.x2[:, 22528:24576], k.x2[:, 24576:26624],
                 k.x2[:, 26624:28672], k.x2[:, 28672:30720]]
        k.qh = [k.x2[:, 8192:10240], k.x2[:, 10240:12288]]
        k.kh = [k.x2[:, 12288:14336], k.x2[:, 14336:16384]]
        k.vtm = [k.x2[:, 16384:18432], k.x2[:, 18432:20480]]
        k.ohst = k.x2[:, 20480:22528]
        aug = sb("aug", [65, 10240], BF16)
        k.bup_aug = aug[0:17, 0:8192]
        k.bdn_aug = aug[0:17, 8192:10240]
        k.bqkv_r = aug[64:65, 0:6144]
        k.bproj_r = aug[64:65, 6144:8192]
        k.ones_r = aug[64:65, 8192:8704]
        k.tup_aug = sb("tup_aug", [17, 512], BF16)
        k.tdn_aug = sb("tdn_aug", [17, 512], BF16)
        k.aup = sb("aup", [128, 256], BF16)
        k.adn = sb("adn", [128, 1024], BF16)
        k.ones_c = sb("ones_c", [128, 1], BF16)
        k.ones_f = sb("ones_f", [1, 128], F32)
        rows = sb("rows", [1, 2048], F32)
        k.cstc = sb("cstc", [128, 2], F32)
        k.mrow = rows[:, 0:512]
        k.vrow = rows[:, 512:1024]
        k.rstd = rows[:, 1024:1536]
        k.sbr = rows[:, 1536:2048]
        nmax4 = sb("nmax4", [128, 4], F32)
        lcol4 = sb("lcol4", [128, 4], F32)
        linv4 = sb("linv4", [128, 4], F32)
        k.nmax = [nmax4[:, i:i + 1] for i in range(4)]
        k.lcol = [lcol4[:, i:i + 1] for i in range(4)]
        k.linv = [linv4[:, i:i + 1] for i in range(4)]
        k.PB1 = ctx.enter_context(nc.psum_tensor("PB1", [128, 2048], F32))
        k.PB2 = ctx.enter_context(nc.psum_tensor("PB2", [128, 2048], F32))

        sem_names = ["pe", "act", "dve", "w0", "w1", "xl", "xs", "st",
                     "qs0", "qs1", "a0", "a1", "a2", "a3",
                     "at0", "at1", "at2", "at3",
                     "qk0", "qk1", "u0", "u1", "u2", "oh",
                     "xg0", "xg1", "xg2", "xg3"]
        for sn in sem_names:
            p.S[sn] = ctx.enter_context(nc.semaphore("s_" + sn))

        k.xres_store_mark = 0
        k.xbf_free = 0
        k.ln_bank_guard_act = 0
        k.ln_bank_guard_dve = 0
        k.x2_free = 0
        k.x2_guards = []
        k.wsl_free = [0, 0]

        emit_p0(k)
        p.wait("sync", "dve")
        p.wait("sync", "qs0", k.p0_done["qs0"])
        p.wait("sync", "qs1", k.p0_done["qs1"])
        p.wait("sync", "st", k.p0_done["smalls"])
        k.x2_free = p.cnt["pe"]
        k.x2_guards = [("qs0", k.p0_done["qs0"]), ("qs1", k.p0_done["qs1"]),
                       ("qk0", p.cnt["qk0"]), ("qk1", p.cnt["qk1"])]

        for s in range(SEQ):
            for t in range(NT):
                emit_p1_tile(k, s, t)
                if stage == "p1dbg" and s == 0 and t == 0:
                    p.wait("gpsimd", "dve", p.cnt["dve"])
                    p.wait("gpsimd", "act", p.cnt["act"])
                    p.wait("gpsimd", "pe", p.cnt["pe"])
                    p.dma("gpsimd", k.DXN.ap(), k.xbf[:], "st")
                    p.dma("gpsimd", k.DX2.ap(), k.x2[:], "st")
                    p.dma("gpsimd", k.DW1.ap(), k.wsl[1][:], "st")
                    p.dma("gpsimd", k.DWT.ap(), k.WdnT.ap()[:, 1792:2048], "st")
                    p.wait("vector", "st", p.cnt["st"])
                    p.wait("scalar", "st", p.cnt["st"])
                    p.wait("sync", "st", p.cnt["st"])
        if stage not in ("p1", "p1dbg"):
            k.head_consumed = [0, 0]
            k.a_store = [p.cnt["a0"], p.cnt["a1"], p.cnt["a2"], p.cnt["a3"]]
            k.prev_at = [p.cnt["at%d" % i] for i in range(4)]
            k.vtb_st0 = p.cnt["qs0"]
            k.vtb_st1 = p.cnt["qs1"]
            k.oh_store = 0
            k.prev_oevac = p.cnt["act"]
            k.prev_exp15 = p.cnt["act"]
            k.otile_free = p.cnt["pe"]
            for s in range(SEQ):
                for t in range(NT):
                    emit_p1b_tile(k, s, t)
                k.head_consumed = [max(k.head_consumed[0], k.x2_free),
                                   max(k.head_consumed[1], k.x2_free)]
                emit_vtm_bounce(k, s)
                for h in range(NH):
                    emit_p2a_head(k, s, h)
                k.x2_free = p.cnt["pe"]
                k.x2_guards = [("a%d" % i, p.cnt["a%d" % i]) for i in range(4)]
                k.x2_guards.append(("oh", p.cnt["oh"]))
                for i in range(4):
                    k.x2_guards.append(("at%d" % i, p.cnt["at%d" % i]))
                for t in range(NT):
                    emit_p2b_tile(k, s, t)

        if stage == "p2a":
            p.wait("gpsimd", "oh", p.cnt["oh"])
            p.wait("gpsimd", "qs0", p.cnt["qs0"])
            p.wait("gpsimd", "qs1", p.cnt["qs1"])
            p.dma("gpsimd", k.DOS.ap(), k.OS.ap(), "st")
            p.dma("gpsimd", k.DQS.ap(), k.QS.ap(), "st")
            p.dma("gpsimd", k.DKS.ap(), k.KS.ap(), "st")
            p.dma("gpsimd", k.DVS.ap(), k.VS.ap(), "st")
            p.dma("gpsimd", k.DAS.ap(), k.AS.ap(), "st")
            p.wait("gpsimd", "st", p.cnt["st"])
        p.wait("gpsimd", "xs", p.cnt["xs"])

        with nc.Block() as block:
            @block.sync
            def _(sync):
                for fn in p.q["sync"]:
                    fn(sync)

            @block.gpsimd
            def _(gpsimd):
                for fn in p.q["gpsimd"]:
                    fn(gpsimd)

            @block.tensor
            def _(tensor):
                for fn in p.q["tensor"]:
                    fn(tensor)

            @block.vector
            def _(vector):
                for fn in p.q["vector"]:
                    fn(vector)

            @block.scalar
            def _(scalar):
                for fn in p.q["scalar"]:
                    fn(scalar)
    return nc


def kernel(**inputs):
    x = np.asarray(inputs["L0_attn_x"])
    nc = build(stage="full")
    in_maps = []
    for c in range(NCORES):
        shard = np.ascontiguousarray(
            x[SEQ * c: SEQ * (c + 1)].transpose(0, 2, 1))
        cst = np.zeros((128, 2), dtype=np.float32)
        cst[:, 0] = EPS
        cst[:, 1] = SCALE
        m = {"x": shard, "cst": cst}
        for name in inputs:
            if name != "L0_attn_x":
                m[name] = np.ascontiguousarray(np.asarray(inputs[name]))
        in_maps.append(m)
    res = run_bass_kernel_spmd(nc, in_maps, core_ids=list(range(NCORES)))
    outs = [np.asarray(r["out"]).transpose(0, 2, 1) for r in res.results]
    return np.concatenate(outs, axis=0).astype(np.float32)


# version 17
# speedup vs baseline: 1.0430x; 1.0430x over previous
import contextlib
from collections import defaultdict

import numpy as np
import ml_dtypes

import concourse.bass as bass
import concourse.mybir as mybir
from concourse.bass_utils import run_bass_kernel_spmd

F32 = mybir.dt.float32
BF16 = mybir.dt.bfloat16
I32 = mybir.dt.int32
AF = mybir.ActivationFunctionType
ALU = mybir.AluOpType
AX = mybir.AxisListType

NCORES = 8
SEQ = 2
L = 2048
DM = 2048
DH = 8192
NH = 16
HD = 128
T = 512
NT = L // T
NDC = DM // 128
NHC = DH // 128
SCALE = float(1.0 / np.sqrt(HD))
EPS = 1e-5

ENGINES = ("sync", "gpsimd", "tensor", "vector", "scalar")


class Plan:

    def __init__(self):
        self.q = {e: [] for e in ENGINES}
        self.cnt = defaultdict(int)
        self.S = {}

    def wait(self, eng, sem, val=None):
        v = self.cnt[sem] if val is None else val
        if v <= 0:
            return
        self.q[eng].append(lambda e, s=sem, vv=v: e.wait_ge(self.S[s], vv))

    def op(self, eng, fn, inc=None, n=1):
        if inc is None:
            self.q[eng].append(fn)
            return None
        self.cnt[inc] += n
        self.q[eng].append(
            lambda e, f=fn, s=inc, k=n: f(e).then_inc(self.S[s], k))
        return self.cnt[inc]

    def dma(self, eng, out, in_, sem, **kw):
        self.cnt[sem] += 16
        self.q[eng].append(
            lambda e, o=out, i=in_, s=sem, kww=dict(kw):
                e.dma_start(o, i, **kww).then_inc(self.S[s], 16))
        return self.cnt[sem]


class K:
    pass


def _psum_slot(k, bank):
    if bank < 4:
        return k.PB1[:, bank * 512:(bank + 1) * 512]
    return k.PB2[:, (bank - 4) * 512:(bank - 4 + 1) * 512]


def emit_p0(k):
    p = k.p
    p.op("vector", lambda e: e.memset(k.ones_r, 1.0), inc="dve")
    p.op("vector", lambda e: e.memset(k.ones_c[:], 1.0), inc="dve")
    p.op("vector", lambda e: e.memset(k.ones_f[:], 1.0), inc="dve")
    p.op("vector", lambda e: e.memset(k.tup_aug[:], 1.0), inc="dve")
    p.op("vector", lambda e: e.memset(k.tdn_aug[:], 0.5), inc="dve")

    p.dma("gpsimd", k.bup_aug[0:16, :], k.IN["L0_ffn_w_up_lora_b"].ap(), "st")
    p.dma("gpsimd", k.bup_aug[16:17, :], k.IN["L0_ffn_b_up"].ap()[None, :], "st")
    p.dma("gpsimd", k.bdn_aug[0:16, :], k.IN["L0_ffn_w_down_lora_b"].ap(), "st")
    p.dma("gpsimd", k.bdn_aug[16:17, :], k.IN["L0_ffn_b_down"].ap()[None, :], "st")
    p.dma("gpsimd", k.aup[:].rearrange("p (dc r) -> p dc r", r=16),
          k.IN["L0_ffn_w_up_lora_a"].ap().rearrange("(dc p) r -> p dc r", p=128), "st")
    p.dma("gpsimd", k.adn[:].rearrange("p (hc r) -> p hc r", r=16),
          k.IN["L0_ffn_w_down_lora_a"].ap().rearrange("(hc p) r -> p hc r", p=128), "st")
    p.dma("gpsimd", k.bqkv_r[:], k.IN["L0_attn_b_qkv"].ap()[None, :], "st")
    p.dma("gpsimd", k.bproj_r[:], k.IN["L0_attn_b_proj"].ap()[None, :], "st")
    k.smalls_done = p.cnt["st"]

    p.dma("sync", k.cstc[:], k.IN["cst"].ap(), "xl")
    upsc = k.xres[:, 0:2048]
    dnsc = k.xres[:, 2048:4096]
    p.dma("sync", upsc.rearrange("p (hc c) -> p hc c", c=32),
          k.IN["L0_ffn_w_up_scale"].ap().rearrange("(hc p) c -> p hc c", p=128), "xl")
    p.dma("sync", dnsc.rearrange("p (dc c) -> p dc c", c=128),
          k.IN["L0_ffn_w_down_scale"].ap().rearrange("(dc p) c -> p dc c", p=128), "xl")
    sc_mark = p.cnt["xl"]
    p.wait("vector", "xl", sc_mark)
    p.op("vector", lambda e: e.tensor_scalar(dnsc, dnsc, 0.5, None, ALU.mult), inc="dve")
    dns_mark = p.cnt["dve"]

    wdq = k.IN["L0_ffn_w_down_q"].ap()
    wuq = k.IN["L0_ffn_w_up_q"].ap()
    dn_in = [k.wsl[0][:].bitcast(I32)[:, 0:2048], k.wsl[0][:].bitcast(I32)[:, 2048:4096],
             k.wsl[1][:].bitcast(I32)[:, 0:2048], k.wsl[1][:].bitcast(I32)[:, 2048:4096]]
    dn_sem = ["w0", "w1", "at2", "at3"]
    dn_out = [k.x2[:, i * 2048:(i + 1) * 2048] for i in range(4)]
    dn_osem = ["a0", "a1", "a2", "a3"]
    up_in = [k.xbf[:].bitcast(I32)[:, 0:2048], k.xbf[:].bitcast(I32)[:, 2048:4096],
             k.x2[:, 20480:28672].bitcast(I32)[:, 0:2048],
             k.x2[:, 20480:28672].bitcast(I32)[:, 2048:4096]]
    up_sem = ["u0", "u1", "xg0", "xg1"]
    up_out = [k.x2[:, 16384 + i * 2048: 16384 + (i + 1) * 2048] for i in range(2)]
    up_osem = ["qk0", "qk1"]
    dn_ld = [0] * 4
    dn_act = [0] * 4
    dn_st = [0] * 4
    up_ld = [0] * 4
    up_act = [0] * 4
    up_st = [0] * 2

    def dn_load(ch):
        c, m = ch // 4, ch % 4
        sl = ch % 4
        if ch >= 4:
            p.wait("sync", "act", dn_act[sl])
        dn_ld[sl] = p.dma("sync", dn_in[sl],
                          wdq[c * 128:(c + 1) * 128, m * 2048:(m + 1) * 2048],
                          dn_sem[sl])

    def dn_comp(ch):
        c, m = ch // 4, ch % 4
        sl = ch % 4
        p.wait("scalar", dn_sem[sl], dn_ld[sl])
        if ch >= 4:
            p.wait("scalar", dn_osem[sl], dn_st[sl])
        p.op("scalar", lambda e, o=dn_out[sl], i=dn_in[sl]:
             e.activation(o, i, AF.Copy, bias=-7.5, scale=1.0), inc="act")
        dn_act[sl] = p.cnt["act"]
        if ch == 0:
            p.wait("vector", "dve", dns_mark)
        p.wait("vector", "act", dn_act[sl])
        sc = dnsc.rearrange("p (dc c2) -> p dc c2", c2=128)[:, c, m * 32:(m + 1) * 32]
        p.op("vector", lambda e, o=dn_out[sl].rearrange("p (a b) -> p a b", b=64),
             s=sc[:, :, None].broadcast_to([128, 32, 64]):
             e.tensor_tensor(o, o, s, ALU.mult), inc="dve")
        p.wait("gpsimd", "dve")
        dn_st[sl] = p.dma("gpsimd",
                          k.WdnN.ap()[c * 128:(c + 1) * 128, m * 2048:(m + 1) * 2048],
                          dn_out[sl], dn_osem[sl])

    def up_load(c):
        sl = c % 4
        if c >= 4:
            p.wait("scalar", "act", up_act[sl])
        p.op("scalar", lambda e, i=up_in[sl], c=c:
             e.dma_start(i, wuq[c * 128:(c + 1) * 128, :]), inc=up_sem[sl], n=16)
        up_ld[sl] = p.cnt[up_sem[sl]]

    def up_comp(c):
        sl = c % 4
        osl = c % 2
        p.wait("scalar", up_sem[sl], up_ld[sl])
        if c >= 2:
            p.wait("scalar", up_osem[osl], up_st[osl])
        p.op("scalar", lambda e, o=up_out[osl], i=up_in[sl]:
             e.activation(o, i, AF.Copy, bias=-7.5, scale=1.0), inc="act")
        up_act[sl] = p.cnt["act"]
        p.wait("vector", "act", up_act[sl])
        sc = upsc.rearrange("p (hc c2) -> p hc c2", c2=32)[:, c, :]
        p.op("vector", lambda e, o=up_out[osl].rearrange("p (a b) -> p a b", b=64),
             s=sc[:, :, None].broadcast_to([128, 32, 64]):
             e.tensor_tensor(o, o, s, ALU.mult), inc="dve")
        p.wait("gpsimd", "dve")
        up_st[osl] = p.dma("gpsimd", k.WupN.ap()[c * 128:(c + 1) * 128, :],
                           up_out[osl], up_osem[osl])

    for step in range(64):
        dn_load(step)
        up_load(step)
        if step >= 3:
            dn_comp(step - 3)
            up_comp(step - 3)
    for step in range(61, 64):
        dn_comp(step)
        up_comp(step)
    dn_nat_done = (p.cnt["a0"], p.cnt["a1"], p.cnt["a2"], p.cnt["a3"])
    up_nat_done = (p.cnt["qk0"], p.cnt["qk1"])

    for i in range(4):
        p.wait("sync", "a%d" % i, dn_nat_done[i])
    ld_mark = [0, 0]
    st2 = [0, 0]
    for hc in range(64):
        par = hc % 2
        if hc >= 2:
            p.wait("sync", "qs%d" % par, st2[par])
        ld_mark[par] = p.dma("sync", k.abuf[par],
                             k.WdnN.ap()[0:2048, hc * 128:(hc + 1) * 128],
                             "at%d" % par, transpose=True)
        p.wait("gpsimd", "at%d" % par, ld_mark[par])
        st2[par] = p.dma("gpsimd", k.WdnT.ap()[hc * 128:(hc + 1) * 128, :],
                         k.abuf[par], "qs%d" % par)

    p.wait("sync", "qk0", up_nat_done[0])
    p.wait("sync", "qk1", up_nat_done[1])
    p.wait("sync", "qs0")
    p.wait("sync", "qs1")
    slots = (k.x2[:, 0:8192], k.x2[:, 8192:16384])
    ld_mark = [0, 0]
    st2 = [0, 0]
    for dc in range(16):
        par = dc % 2
        if dc >= 2:
            p.wait("sync", "qs%d" % par, st2[par])
        ld_mark[par] = p.dma("sync", slots[par],
                             k.WupN.ap()[0:8192, dc * 128:(dc + 1) * 128],
                             "at%d" % par, transpose=True)
        p.wait("gpsimd", "at%d" % par, ld_mark[par])
        st2[par] = p.dma("gpsimd", k.WupT.ap()[dc * 128:(dc + 1) * 128, :],
                         slots[par], "qs%d" % par)

    p.dma("gpsimd", k.WqkvB.ap(), k.IN["L0_attn_w_qkv"].ap(), "u2")
    p.dma("gpsimd", k.WprojB.ap(), k.IN["L0_attn_w_proj"].ap(), "u2")
    k.cast_mark = p.cnt["u2"]

    k.p0_done = dict(qs0=p.cnt["qs0"], qs1=p.cnt["qs1"], smalls=k.smalls_done)


def emit_ln(k, pe_guard=None, xl_marks=None):
    p = k.p
    cast_marks = []
    for c in range(NDC):
        if xl_marks is not None and c % 4 == 0:
            p.wait("vector", xl_marks[c // 4][0], xl_marks[c // 4][1])
        p.op("vector", lambda e, c=c: e.tensor_copy(
            k.xbf[:, c * T:(c + 1) * T], k.xres[:, c * T:(c + 1) * T]), inc="dve")
        cast_marks.append(p.cnt["dve"])
    cast_mark = p.cnt["dve"]
    LNm = k.PB1[0:1, 0:512]
    LNs2 = k.PB1[0:1, 512:1024]
    if pe_guard:
        for s, v in pe_guard.items():
            p.wait("tensor", s, v)
    for c in range(NDC):
        p.wait("tensor", "dve", cast_marks[c])
        p.op("tensor", lambda e, c=c: e.matmul(
            LNm, k.ones_c[:], k.xbf[:, c * T:(c + 1) * T],
            start=(c == 0), stop=(c == NDC - 1)),
            inc=("pe" if c == NDC - 1 else None))
    for c in range(NDC):
        p.wait("scalar", "dve", cast_marks[c])
        if c >= 2:
            p.wait("scalar", "pe")
        p.op("scalar", lambda e, c=c: e.activation(
            k.sqb[:, (c % 2) * 512:(c % 2 + 1) * 512],
            k.xbf[:, c * T:(c + 1) * T], AF.Square), inc="act")
        p.wait("tensor", "act")
        p.op("tensor", lambda e, c=c: e.matmul(
            LNs2, k.ones_c[:], k.sqb[:, (c % 2) * 512:(c % 2 + 1) * 512],
            start=(c == 0), stop=(c == NDC - 1)), inc="pe")
    stats_mark = p.cnt["pe"]
    p.wait("vector", "pe", stats_mark)
    p.op("vector", lambda e: e.tensor_scalar(k.mrow, LNm, 1.0 / DM, None, ALU.mult), inc="dve")
    p.op("vector", lambda e: e.tensor_scalar(k.vrow, LNs2, 1.0 / DM, None, ALU.mult), inc="dve")
    p.op("vector", lambda e: e.tensor_tensor(k.sbr, k.mrow, k.mrow, ALU.mult), inc="dve")
    p.op("vector", lambda e: e.tensor_tensor(k.vrow, k.vrow, k.sbr, ALU.subtract), inc="dve")
    p.wait("scalar", "dve")
    p.op("scalar", lambda e: e.activation(k.rstd, k.vrow, AF.Sqrt, bias=k.cstc[0:1, 0:1]), inc="act")
    p.wait("vector", "act")
    p.op("vector", lambda e: e.reciprocal(k.vrow, k.rstd), inc="dve")
    p.wait("vector", "dve")
    p.op("vector", lambda e: e.scalar_tensor_tensor(
        k.sbr, k.mrow, -1.0, k.vrow, ALU.mult, ALU.mult), inc="dve")
    row_mark = p.cnt["dve"]
    SAB = k.PB1[:, 1024:1536]
    SBB = k.PB1[:, 1536:2048]
    p.wait("tensor", "dve", row_mark)
    p.op("tensor", lambda e: e.matmul(SAB, k.ones_f[:], k.vrow, start=True, stop=True), inc=None)
    p.op("tensor", lambda e: e.matmul(SBB, k.ones_f[:], k.sbr, start=True, stop=True), inc="pe")
    bc_mark = p.cnt["pe"]
    p.wait("vector", "pe", bc_mark)
    for c in range(NDC):
        p.op("vector", lambda e, c=c: e.tensor_tensor(
            k.xbf[:, c * T:(c + 1) * T], k.xres[:, c * T:(c + 1) * T], SAB, ALU.mult), inc="dve")
        p.op("vector", lambda e, c=c: e.tensor_tensor(
            k.xbf[:, c * T:(c + 1) * T], k.xbf[:, c * T:(c + 1) * T], SBB, ALU.add), inc="dve")
    k.xn_mark = p.cnt["dve"]


def emit_ffn(k):
    p = k.p
    TUP = k.PB2[0:16, 0:512]
    p.wait("tensor", "dve", k.xn_mark)
    for dc in range(NDC):
        p.op("tensor", lambda e, dc=dc: e.matmul(
            TUP, k.aup[:, dc * 16:(dc + 1) * 16], k.xbf[:, dc * T:(dc + 1) * T],
            start=(dc == 0), stop=(dc == NDC - 1)),
            inc=("pe" if dc == NDC - 1 else None))
    p.wait("vector", "pe")
    p.op("vector", lambda e: e.tensor_copy(k.tup_aug[0:16, :], TUP), inc="dve")
    tup_mark = p.cnt["dve"]

    evac_mark = [0] * 8
    wsl_free = list(k.wsl_free)
    wview = [w[:].rearrange("p (dc h) -> p dc h", h=512) for w in k.wsl]
    for u in range(16):
        par = u % 2
        p.wait("sync", "pe", wsl_free[par])
        wm = p.dma("sync", wview[par],
                   k.WupT.ap().rearrange("(dc p) h -> p dc h", p=128)
                   [:, :, u * 512:(u + 1) * 512], "w%d" % par)
        for hcl in range(4):
            i = u * 4 + hcl
            bank = i % 8
            p.wait("tensor", "w%d" % par, wm)
            if i == 0:
                p.wait("tensor", "dve", tup_mark)
            if i >= 8:
                p.wait("tensor", "act", evac_mark[bank])
            slot = _psum_slot(k, bank)
            for dc in range(NDC):
                p.op("tensor", lambda e, hcl=hcl, dc=dc, slot=slot, par=par:
                     e.matmul(slot, k.wsl[par][:, dc * 512 + hcl * 128: dc * 512 + (hcl + 1) * 128],
                              k.xbf[:, dc * T:(dc + 1) * T],
                              start=(dc == 0), stop=False), inc=None)
            p.op("tensor", lambda e, i=i, slot=slot:
                 e.matmul(slot, k.bup_aug[0:17, i * 128:(i + 1) * 128],
                          k.tup_aug[0:17, :], start=False, stop=True), inc="pe")
            pe_m = p.cnt["pe"]
            p.wait("scalar", "pe", pe_m)
            if i == 0:
                p.wait("scalar", "pe", k.x2_free)
                for gs, gv in k.x2_guards:
                    p.wait("scalar", gs, gv)
                k.x2_guards = []
            p.op("scalar", lambda e, i=i, slot=slot: e.activation(
                k.x2[:, i * T:(i + 1) * T], slot, AF.Relu), inc="act")
            evac_mark[bank] = p.cnt["act"]
        wsl_free[par] = p.cnt["pe"]

    x2_done = p.cnt["act"]
    TDN = k.PB2[0:16, 0:512]
    p.wait("tensor", "act", x2_done)
    for hc in range(NHC):
        p.op("tensor", lambda e, hc=hc: e.matmul(
            TDN, k.adn[:, hc * 16:(hc + 1) * 16], k.x2[:, hc * T:(hc + 1) * T],
            start=(hc == 0), stop=(hc == NHC - 1)),
            inc=("pe" if hc == NHC - 1 else None))
    p.wait("scalar", "pe")
    p.op("scalar", lambda e: e.activation(k.tdn_aug[0:16, :], TDN, AF.Copy, scale=0.5), inc="act")
    tdn_mark = p.cnt["act"]

    dvac_mark = [0] * 8
    wview2 = [w[:].rearrange("p (hc d) -> p hc d", d=128) for w in k.wsl]
    for sl in range(16):
        oc = sl
        bank = oc % 8
        par = sl % 2
        p.wait("sync", "pe", wsl_free[par])
        wm = p.dma("sync", wview2[par],
                   k.WdnT.ap().rearrange("(hc p) d -> p hc d", p=128)
                   [:, :, oc * 128:(oc + 1) * 128], "w%d" % par)
        p.wait("tensor", "w%d" % par, wm)
        if sl == 0:
            p.wait("tensor", "act", tdn_mark)
        if oc >= 8:
            p.wait("tensor", "dve", dvac_mark[bank])
        slot = _psum_slot(k, bank)
        for hc in range(NHC):
            p.op("tensor", lambda e, hc=hc, slot=slot, par=par:
                 e.matmul(slot, k.wsl[par][:, hc * 128:(hc + 1) * 128],
                          k.x2[:, hc * T:(hc + 1) * T],
                          start=(hc == 0), stop=False), inc=None)
        p.op("tensor", lambda e, oc=oc, slot=slot:
             e.matmul(slot, k.bdn_aug[0:17, oc * 128:(oc + 1) * 128],
                      k.tdn_aug[0:17, :], start=False, stop=True), inc="pe")
        pe_m = p.cnt["pe"]
        p.wait("vector", "pe", pe_m)
        p.op("vector", lambda e, oc=oc, slot=slot: e.tensor_tensor(
            k.xres[:, oc * T:(oc + 1) * T], slot,
            k.xres[:, oc * T:(oc + 1) * T], ALU.add), inc="dve")
        dvac_mark[bank] = p.cnt["dve"]
        wsl_free[par] = p.cnt["pe"]
    k.wsl_free = wsl_free
    k.x2_free = p.cnt["pe"]
    k.ffn_done = p.cnt["dve"]


def emit_p1_tile(k, s, t):
    p = k.p
    xv = k.xres[:].rearrange("p (dc t) -> p dc t", t=T)
    p.wait("sync", "xs", k.xres_store_mark)
    src_ap = k.IN["x"].ap()[s].rearrange("(dc p) t -> p dc t", p=128)
    xl_marks = []
    for g in range(4):
        xl_marks.append(("xg%d" % g,
                         p.dma("sync", xv[:, g * 4:(g + 1) * 4, :],
                               src_ap[:, g * 4:(g + 1) * 4, t * T:(t + 1) * T],
                               "xg%d" % g)))
    p.wait("vector", "pe", k.xbf_free)
    emit_ln(k, pe_guard={"act": k.ln_bank_guard_act, "dve": k.ln_bank_guard_dve},
            xl_marks=xl_marks)
    emit_ffn(k)
    k.xbf_free = p.cnt["pe"]
    k.ln_bank_guard_act = p.cnt["act"]
    k.ln_bank_guard_dve = p.cnt["dve"]
    p.wait("gpsimd", "dve", k.ffn_done)
    dst = (k.OUT.ap() if k.stage in ("p1", "p1dbg") else k.X1.ap())
    k.xres_store_mark = p.dma(
        "gpsimd", dst[s].rearrange("(dc p) t -> p dc t", p=128)[:, :, t * T:(t + 1) * T],
        xv, "xs")


def emit_p1b_tile(k, s, t):
    p = k.p
    xv = k.xres[:].rearrange("p (dc t) -> p dc t", t=T)
    p.wait("sync", "xs", k.xres_store_mark)
    src_ap = k.X1.ap()[s].rearrange("(dc p) t -> p dc t", p=128)
    xl_marks = []
    for g in range(4):
        xl_marks.append(("xg%d" % g,
                         p.dma("sync", xv[:, g * 4:(g + 1) * 4, :],
                               src_ap[:, g * 4:(g + 1) * 4, t * T:(t + 1) * T],
                               "xg%d" % g)))
    p.wait("vector", "pe", k.xbf_free)
    emit_ln(k, pe_guard={"act": k.ln_bank_guard_act, "dve": k.ln_bank_guard_dve},
            xl_marks=xl_marks)

    wsl_free = list(k.wsl_free)
    qs_mark = [0, 0]
    targets = (k.QS, k.QS, k.KS, k.KS, k.VS, k.VS)
    wview = [w[:].rearrange("p (dc o) -> p dc o", o=512) for w in k.wsl]
    evac_mark = [0] * 8
    first_evac = True
    for sl in range(12):
        par = sl % 2
        wave = sl // 2
        spar = wave % 2
        p.wait("sync", "pe", wsl_free[par])
        if sl < 2:
            p.wait("sync", "u2", k.cast_mark)
        wm = p.dma("sync", wview[par],
                   k.WqkvB.ap().rearrange("(dc p) o -> p dc o", p=128)
                   [:, :, sl * 512:(sl + 1) * 512], "w%d" % par)
        for ocl in range(4):
            oc = sl * 4 + ocl
            bank = oc % 8
            p.wait("tensor", "w%d" % par, wm)
            if oc == 0:
                p.wait("tensor", "dve", k.xn_mark)
            if oc >= 8:
                p.wait("tensor", "act", evac_mark[bank])
            slot = _psum_slot(k, bank)
            for dc in range(NDC):
                p.op("tensor", lambda e, dc=dc, ocl=ocl, slot=slot, par=par:
                     e.matmul(slot, k.wsl[par][:, dc * 512 + ocl * 128: dc * 512 + (ocl + 1) * 128],
                              k.xbf[:, dc * T:(dc + 1) * T],
                              start=(dc == 0), stop=False), inc=None)
            p.op("tensor", lambda e, oc=oc, slot=slot:
                 e.matmul(slot, k.bqkv_r[0:1, oc * 128:(oc + 1) * 128],
                          k.ones_r[0:1, 0:T], start=False, stop=True), inc="pe")
            pe_m = p.cnt["pe"]
            p.wait("scalar", "pe", pe_m)
            if first_evac:
                p.wait("scalar", "pe", k.x2_free)
                for gs, gv in k.x2_guards:
                    p.wait("scalar", gs, gv)
                k.x2_guards = []
                first_evac = False
            if sl % 2 == 0 and ocl == 0:
                p.wait("scalar", "qs%d" % spar, qs_mark[spar])
            sc = k.cstc[:, 1:2] if oc < 16 else 1.0
            pos = (sl % 2) * 4 + ocl
            p.op("scalar", lambda e, pos=pos, slot=slot, spar=spar, sc=sc:
                 e.activation(k.qst[spar][:, pos * T:(pos + 1) * T], slot,
                              AF.Copy, scale=sc), inc="act")
            evac_mark[bank] = p.cnt["act"]
        wsl_free[par] = p.cnt["pe"]
        if sl % 2 == 1:
            p.wait("gpsimd", "act", p.cnt["act"])
            tgt = targets[wave]
            half = wave % 2
            qs_mark[spar] = p.dma(
                "gpsimd",
                tgt.ap()[s].rearrange("(oc p) t -> p oc t", p=128)
                [:, half * 8:(half + 1) * 8, t * T:(t + 1) * T],
                k.qst[spar].rearrange("p (o t) -> p o t", t=T), "qs%d" % spar)
        k.wsl_free = wsl_free
    k.xbf_free = p.cnt["pe"]
    k.ln_bank_guard_act = p.cnt["act"]
    k.ln_bank_guard_dve = p.cnt["dve"]
    k.qkv_spill_marks = (p.cnt["qs0"], p.cnt["qs1"])
    k.xres_store_mark = p.cnt["xs"]


def emit_vtm_bounce(k, s):
    p = k.p
    p.wait("sync", "pe", k.x2_free)
    for gs, gv in k.x2_guards:
        p.wait("sync", gs, gv)
    p.wait("sync", "qs0", k.qkv_spill_marks[0])
    p.wait("sync", "qs1", k.qkv_spill_marks[1])
    ld = [0, 0]
    st = [k.vtb_st0, k.vtb_st1]
    for tc in range(16):
        par = tc % 2
        if tc >= 2:
            p.wait("sync", "qs%d" % par, st[par])
        ld[par] = p.dma("sync", k.atb[par],
                        k.VS.ap()[s][0:2048, tc * 128:(tc + 1) * 128],
                        "at%d" % par, transpose=True)
        p.wait("gpsimd", "at%d" % par, ld[par])
        st[par] = p.dma("gpsimd", k.VTM.ap()[tc * 128:(tc + 1) * 128, :],
                        k.atb[par], "qs%d" % par)
    k.vtb_st0, k.vtb_st1 = st
    k.vtb_done = (p.cnt["qs0"], p.cnt["qs1"])
    k.prev_at = [p.cnt["at%d" % i] for i in range(4)]


def emit_p2a_head(k, s, h):
    p = k.p
    par = h % 2
    p.wait("sync", "pe", k.head_consumed[par])
    if h == 0:
        p.wait("sync", "qs0", k.vtb_done[0])
        p.wait("sync", "qs1", k.vtb_done[1])
    p.dma("sync", k.qh[par], k.QS.ap()[s][h * 128:(h + 1) * 128, :], "qk%d" % par)
    p.dma("sync", k.kh[par], k.KS.ap()[s][h * 128:(h + 1) * 128, :], "qk%d" % par)
    p.dma("sync", k.vtm[par][:].rearrange("p (kc j) -> p kc j", j=128),
          k.VTM.ap().rearrange("(kc p) f -> p kc f", p=128)[:, :, h * 128:(h + 1) * 128],
          "qk%d" % par)
    ld_mark = p.cnt["qk%d" % par]

    exp_mark = [0, 0]
    a_store = list(k.a_store)
    for i in range(4):
        p.wait("gpsimd", "at%d" % i, k.prev_at[i])

    def emit_norm(qc):
        qb = qc % 4
        p.op("vector", lambda e, qb=qb: e.reciprocal(k.linv[qb][:], k.lcol[qb][:]), inc="dve")
        p.wait("vector", "dve")
        p.op("vector", lambda e, qb=qb: e.tensor_scalar(
            k.abuf[qb], k.abuf[qb], k.linv[qb][:], None, ALU.mult), inc="dve")
        p.wait("gpsimd", "dve")
        a_store[qb] = p.dma("gpsimd", k.AS.ap()[qc * 128:(qc + 1) * 128, :],
                            k.abuf[qb], "a%d" % qb)

    for qc in range(16):
        qpar = qc % 2
        qb = qc % 4
        PB = k.PB1 if qpar == 0 else k.PB2
        if qc == 0:
            p.wait("tensor", "qk%d" % par, ld_mark)
        if qc >= 2:
            p.wait("tensor", "act", exp_mark[qpar])
        elif qc == 0:
            p.wait("tensor", "act", k.prev_oevac)
        else:
            p.wait("tensor", "act", k.prev_exp15)
        for kr in range(4):
            p.op("tensor", lambda e, qc=qc, kr=kr, PB=PB, par=par:
                 e.matmul(PB[:, kr * 512:(kr + 1) * 512],
                          k.qh[par][:, qc * 128:(qc + 1) * 128],
                          k.kh[par][:, kr * 512:(kr + 1) * 512],
                          start=True, stop=True),
                 inc=("pe" if kr == 3 else None))
        p.wait("vector", "pe")
        p.op("vector", lambda e, qb=qb, PB=PB:
             e.reduce_max(k.nmax[qb][:], PB[:, :], AX.X, negate=True), inc="dve")
        max_mark = p.cnt["dve"]
        p.wait("scalar", "dve", max_mark)
        p.wait("scalar", "a%d" % qb, a_store[qb])
        p.op("scalar", lambda e, qb=qb, PB=PB: e.activation(
            k.abuf[qb], PB[:, :], AF.Exp, bias=k.nmax[qb][:],
            accum_out=k.lcol[qb][:]), inc="act")
        exp_mark[qpar] = p.cnt["act"]
        if qc == 15:
            k.prev_exp15 = exp_mark[qpar]
        if qc > 0:
            p.wait("vector", "act", exp_mark[(qc - 1) % 2])
            emit_norm(qc - 1)
    p.wait("vector", "act", exp_mark[15 % 2])
    emit_norm(15)
    k.a_store = a_store

    at_mark = [0, 0, 0, 0]
    av_pe = [0, 0, 0, 0]
    exp_all = p.cnt["act"]
    at_q = ("sync", "scalar")
    for kc in range(16):
        slot = kc % 4
        q = at_q[kc % 2]
        if kc >= 4:
            p.wait(q, "pe", av_pe[slot])
        else:
            for i in range(4):
                p.wait(q, "a%d" % i, a_store[i])
            if h == 0:
                p.wait(q, "qs0", k.vtb_done[0])
                p.wait(q, "qs1", k.vtb_done[1])
        at_mark[slot] = p.dma(q, k.atb[slot],
                              k.AS.ap()[0:2048, kc * 128:(kc + 1) * 128],
                              "at%d" % slot, transpose=True)
        p.wait("tensor", "at%d" % slot, at_mark[slot])
        if kc == 0:
            p.wait("tensor", "act", exp_all)
        for qr in range(4):
            p.op("tensor", lambda e, kc=kc, qr=qr, par=par, slot=slot:
                 e.matmul(k.PB1[:, qr * 512:(qr + 1) * 512],
                          k.vtm[par][:, kc * 128:(kc + 1) * 128],
                          k.atb[slot][:, qr * 512:(qr + 1) * 512],
                          start=(kc == 0), stop=(kc == 15),
                          skip_group_check=True),
                 inc=("pe" if qr == 3 else None))
        av_pe[slot] = p.cnt["pe"]
    k.head_consumed[par] = p.cnt["pe"]
    k.prev_at = [p.cnt["at%d" % i] for i in range(4)]
    p.wait("scalar", "pe")
    p.wait("scalar", "oh", k.oh_store)
    p.op("scalar", lambda e: e.activation(k.ohst, k.PB1[:, :], AF.Copy), inc="act")
    k.prev_oevac = p.cnt["act"]
    p.wait("gpsimd", "act", k.prev_oevac)
    k.oh_store = p.dma("gpsimd", k.OS.ap()[s][h * 128:(h + 1) * 128, :], k.ohst, "oh")


def emit_p2b_tile(k, s, t):
    p = k.p
    xv = k.xres[:].rearrange("p (dc t) -> p dc t", t=T)
    p.wait("sync", "xs", k.xres_store_mark)
    p.wait("sync", "dve", k.ln_bank_guard_dve)
    p.wait("sync", "oh", k.oh_store)
    p.wait("sync", "pe", k.otile_free)
    om = p.dma("sync", k.otile.rearrange("p (fc t) -> p fc t", t=T),
               k.OS.ap()[s].rearrange("(fc p) t -> p fc t", p=128)
               [:, :, t * T:(t + 1) * T], "xl")
    src_ap = k.X1.ap()[s].rearrange("(dc p) t -> p dc t", p=128)
    x1_marks = []
    for g in range(4):
        x1_marks.append(("xg%d" % g,
                         p.dma("sync", xv[:, g * 4:(g + 1) * 4, :],
                               src_ap[:, g * 4:(g + 1) * 4, t * T:(t + 1) * T],
                               "xg%d" % g)))

    wsl_free = list(k.wsl_free)
    proj_dvac = [0] * 8
    wview = [w[:].rearrange("p (dc o) -> p dc o", o=512) for w in k.wsl]
    for sl in range(4):
        par = sl % 2
        p.wait("sync", "pe", wsl_free[par])
        if sl < 2:
            p.wait("sync", "u2", k.cast_mark)
        wm = p.dma("sync", wview[par],
                   k.WprojB.ap().rearrange("(dc p) o -> p dc o", p=128)
                   [:, :, sl * 512:(sl + 1) * 512], "w%d" % par)
        for ocl in range(4):
            oc = sl * 4 + ocl
            bank = oc % 8
            p.wait("tensor", "w%d" % par, wm)
            if oc == 0:
                p.wait("tensor", "xl", om)
                p.wait("tensor", "act", k.prev_oevac)
                p.wait("tensor", "dve", k.ln_bank_guard_dve)
            if oc >= 8:
                p.wait("tensor", "dve", proj_dvac[bank])
            slot = _psum_slot(k, bank)
            for dc in range(NDC):
                p.op("tensor", lambda e, dc=dc, ocl=ocl, slot=slot, par=par:
                     e.matmul(slot, k.wsl[par][:, dc * 512 + ocl * 128: dc * 512 + (ocl + 1) * 128],
                              k.otile[:, dc * T:(dc + 1) * T],
                              start=(dc == 0), stop=False), inc=None)
            p.op("tensor", lambda e, oc=oc, slot=slot:
                 e.matmul(slot, k.bproj_r[0:1, oc * 128:(oc + 1) * 128],
                          k.ones_r[0:1, 0:T], start=False, stop=True), inc="pe")
            pe_m = p.cnt["pe"]
            p.wait("vector", "pe", pe_m)
            if oc == 0:
                for gs, gv in x1_marks:
                    p.wait("vector", gs, gv)
            p.op("vector", lambda e, oc=oc, slot=slot: e.tensor_tensor(
                k.xres[:, oc * T:(oc + 1) * T], slot,
                k.xres[:, oc * T:(oc + 1) * T], ALU.add), inc="dve")
            proj_dvac[bank] = p.cnt["dve"]
        wsl_free[par] = p.cnt["pe"]
    k.wsl_free = wsl_free
    k.otile_free = p.cnt["pe"]
    proj_done = p.cnt["dve"]

    if k.stage == "p2a":
        p.wait("gpsimd", "dve", proj_done)
        k.xres_store_mark = p.dma(
            "gpsimd",
            k.OUT.ap()[s].rearrange("(dc p) t -> p dc t", p=128)[:, :, t * T:(t + 1) * T],
            xv, "xs")
        return

    p.wait("vector", "pe", k.xbf_free)
    emit_ln(k, pe_guard={"act": k.ln_bank_guard_act, "dve": k.ln_bank_guard_dve})
    emit_ffn(k)
    k.xbf_free = p.cnt["pe"]
    k.ln_bank_guard_act = p.cnt["act"]
    k.ln_bank_guard_dve = p.cnt["dve"]
    p.wait("gpsimd", "dve", k.ffn_done)
    k.xres_store_mark = p.dma(
        "gpsimd",
        k.OUT.ap()[s].rearrange("(dc p) t -> p dc t", p=128)[:, :, t * T:(t + 1) * T],
        xv, "xs")


def build(stage="full"):
    nc = bass.Bass("TRN2", target_bir_lowering=False)
    k = K()
    k.stage = stage
    k.p = Plan()
    p = k.p

    k.IN = {}
    k.IN["x"] = nc.dram_tensor("x", [SEQ, DM, L], F32, kind="ExternalInput")
    k.IN["cst"] = nc.dram_tensor("cst", [128, 2], F32, kind="ExternalInput")
    for name, shape, dt in [
        ("L0_attn_w_qkv", [DM, 3 * DM], F32), ("L0_attn_b_qkv", [3 * DM], F32),
        ("L0_attn_w_proj", [DM, DM], F32), ("L0_attn_b_proj", [DM], F32),
        ("L0_ffn_w_up_q", [DH, DM], I32), ("L0_ffn_w_up_scale", [DH, DM // 64], F32),
        ("L0_ffn_b_up", [DH], F32), ("L0_ffn_w_up_lora_a", [DM, 16], F32),
        ("L0_ffn_w_up_lora_b", [16, DH], F32), ("L0_ffn_w_down_q", [DM, DH], I32),
        ("L0_ffn_w_down_scale", [DM, DH // 64], F32), ("L0_ffn_b_down", [DM], F32),
        ("L0_ffn_w_down_lora_a", [DH, 16], F32), ("L0_ffn_w_down_lora_b", [16, DM], F32),
    ]:
        k.IN[name] = nc.dram_tensor(name, shape, dt, kind="ExternalInput")
    k.OUT = nc.dram_tensor("out", [SEQ, DM, L], F32, kind="ExternalOutput")
    if stage == "p2a":
        k.DOS = nc.dram_tensor("dos", [SEQ, DM, L], BF16, kind="ExternalOutput")
        k.DQS = nc.dram_tensor("dqs", [SEQ, DM, L], BF16, kind="ExternalOutput")
        k.DKS = nc.dram_tensor("dks", [SEQ, DM, L], BF16, kind="ExternalOutput")
        k.DVS = nc.dram_tensor("dvs", [SEQ, DM, L], BF16, kind="ExternalOutput")
        k.DAS = nc.dram_tensor("das", [L, L], BF16, kind="ExternalOutput")
    if stage == "p1dbg":
        k.DXN = nc.dram_tensor("dxn", [128, 8192], BF16, kind="ExternalOutput")
        k.DX2 = nc.dram_tensor("dx2", [128, 32768], BF16, kind="ExternalOutput")
        k.DW1 = nc.dram_tensor("dw1", [128, 8192], BF16, kind="ExternalOutput")
        k.DWT = nc.dram_tensor("dwt", [DH, 256], BF16, kind="ExternalOutput")

    k.WqkvB = nc.dram_tensor("WqkvB", [DM, 3 * DM], BF16)
    k.WprojB = nc.dram_tensor("WprojB", [DM, DM], BF16)
    k.WupN = nc.dram_tensor("WupN", [DH, DM], BF16)
    k.WupT = nc.dram_tensor("WupT", [DM, DH], BF16)
    k.WdnN = nc.dram_tensor("WdnN", [DM, DH], BF16)
    k.WdnT = nc.dram_tensor("WdnT", [DH, DM], BF16)
    k.X1 = nc.dram_tensor("X1", [SEQ, DM, L], F32)
    k.QS = nc.dram_tensor("QS", [SEQ, DM, L], BF16)
    k.KS = nc.dram_tensor("KS", [SEQ, DM, L], BF16)
    k.VS = nc.dram_tensor("VS", [SEQ, DM, L], BF16)
    k.AS = nc.dram_tensor("AS", [L, L], BF16)
    k.VTM = nc.dram_tensor("VTM", [L, DM], BF16)
    k.OS = nc.dram_tensor("OS", [SEQ, DM, L], BF16)

    ctx = contextlib.ExitStack()
    sb = lambda name, shape, dt: ctx.enter_context(nc.sbuf_tensor(name, shape, dt))
    with ctx:
        k.wsl = [sb("wsl0", [128, 8192], BF16), sb("wsl1", [128, 8192], BF16)]
        k.xres = sb("xres", [128, 8192], F32)
        k.xbf = sb("xbf", [128, 8192], BF16)
        k.sqb = sb("sqb", [128, 1024], BF16)
        k.x2 = sb("x2", [128, 32768], BF16)
        k.otile = k.x2[:, 24576:32768]
        k.qst = [k.x2[:, 0:4096], k.x2[:, 4096:8192]]
        k.abuf = [k.x2[:, 0:2048], k.x2[:, 2048:4096],
                  k.x2[:, 4096:6144], k.x2[:, 6144:8192]]
        k.atb = [k.x2[:, 22528:24576], k.x2[:, 24576:26624],
                 k.x2[:, 26624:28672], k.x2[:, 28672:30720]]
        k.qh = [k.x2[:, 8192:10240], k.x2[:, 10240:12288]]
        k.kh = [k.x2[:, 12288:14336], k.x2[:, 14336:16384]]
        k.vtm = [k.x2[:, 16384:18432], k.x2[:, 18432:20480]]
        k.ohst = k.x2[:, 20480:22528]
        aug = sb("aug", [65, 10240], BF16)
        k.bup_aug = aug[0:17, 0:8192]
        k.bdn_aug = aug[0:17, 8192:10240]
        k.bqkv_r = aug[64:65, 0:6144]
        k.bproj_r = aug[64:65, 6144:8192]
        k.ones_r = aug[64:65, 8192:8704]
        k.tup_aug = sb("tup_aug", [17, 512], BF16)
        k.tdn_aug = sb("tdn_aug", [17, 512], BF16)
        k.aup = sb("aup", [128, 256], BF16)
        k.adn = sb("adn", [128, 1024], BF16)
        k.ones_c = sb("ones_c", [128, 1], BF16)
        k.ones_f = sb("ones_f", [1, 128], F32)
        rows = sb("rows", [1, 2048], F32)
        k.cstc = sb("cstc", [128, 2], F32)
        k.mrow = rows[:, 0:512]
        k.vrow = rows[:, 512:1024]
        k.rstd = rows[:, 1024:1536]
        k.sbr = rows[:, 1536:2048]
        nmax4 = sb("nmax4", [128, 4], F32)
        lcol4 = sb("lcol4", [128, 4], F32)
        linv4 = sb("linv4", [128, 4], F32)
        k.nmax = [nmax4[:, i:i + 1] for i in range(4)]
        k.lcol = [lcol4[:, i:i + 1] for i in range(4)]
        k.linv = [linv4[:, i:i + 1] for i in range(4)]
        k.PB1 = ctx.enter_context(nc.psum_tensor("PB1", [128, 2048], F32))
        k.PB2 = ctx.enter_context(nc.psum_tensor("PB2", [128, 2048], F32))

        sem_names = ["pe", "act", "dve", "w0", "w1", "xl", "xs", "st",
                     "qs0", "qs1", "a0", "a1", "a2", "a3",
                     "at0", "at1", "at2", "at3",
                     "qk0", "qk1", "u0", "u1", "u2", "oh",
                     "xg0", "xg1", "xg2", "xg3"]
        for sn in sem_names:
            p.S[sn] = ctx.enter_context(nc.semaphore("s_" + sn))

        k.xres_store_mark = 0
        k.xbf_free = 0
        k.ln_bank_guard_act = 0
        k.ln_bank_guard_dve = 0
        k.x2_free = 0
        k.x2_guards = []
        k.wsl_free = [0, 0]

        emit_p0(k)
        p.wait("sync", "dve")
        p.wait("sync", "qs0", k.p0_done["qs0"])
        p.wait("sync", "qs1", k.p0_done["qs1"])
        p.wait("sync", "st", k.p0_done["smalls"])
        k.x2_free = p.cnt["pe"]
        k.x2_guards = [("qs0", k.p0_done["qs0"]), ("qs1", k.p0_done["qs1"]),
                       ("qk0", p.cnt["qk0"]), ("qk1", p.cnt["qk1"])]

        for s in range(SEQ):
            for t in range(NT):
                emit_p1_tile(k, s, t)
                if stage == "p1dbg" and s == 0 and t == 0:
                    p.wait("gpsimd", "dve", p.cnt["dve"])
                    p.wait("gpsimd", "act", p.cnt["act"])
                    p.wait("gpsimd", "pe", p.cnt["pe"])
                    p.dma("gpsimd", k.DXN.ap(), k.xbf[:], "st")
                    p.dma("gpsimd", k.DX2.ap(), k.x2[:], "st")
                    p.dma("gpsimd", k.DW1.ap(), k.wsl[1][:], "st")
                    p.dma("gpsimd", k.DWT.ap(), k.WdnT.ap()[:, 1792:2048], "st")
                    p.wait("vector", "st", p.cnt["st"])
                    p.wait("scalar", "st", p.cnt["st"])
                    p.wait("sync", "st", p.cnt["st"])
        if stage not in ("p1", "p1dbg"):
            k.head_consumed = [0, 0]
            k.a_store = [p.cnt["a0"], p.cnt["a1"], p.cnt["a2"], p.cnt["a3"]]
            k.prev_at = [p.cnt["at%d" % i] for i in range(4)]
            k.vtb_st0 = p.cnt["qs0"]
            k.vtb_st1 = p.cnt["qs1"]
            k.oh_store = 0
            k.prev_oevac = p.cnt["act"]
            k.prev_exp15 = p.cnt["act"]
            k.otile_free = p.cnt["pe"]
            for s in range(SEQ):
                for t in range(NT):
                    emit_p1b_tile(k, s, t)
                k.head_consumed = [max(k.head_consumed[0], k.x2_free),
                                   max(k.head_consumed[1], k.x2_free)]
                emit_vtm_bounce(k, s)
                for h in range(NH):
                    emit_p2a_head(k, s, h)
                k.x2_free = p.cnt["pe"]
                k.x2_guards = [("a%d" % i, p.cnt["a%d" % i]) for i in range(4)]
                k.x2_guards.append(("oh", p.cnt["oh"]))
                for i in range(4):
                    k.x2_guards.append(("at%d" % i, p.cnt["at%d" % i]))
                for t in range(NT):
                    emit_p2b_tile(k, s, t)

        if stage == "p2a":
            p.wait("gpsimd", "oh", p.cnt["oh"])
            p.wait("gpsimd", "qs0", p.cnt["qs0"])
            p.wait("gpsimd", "qs1", p.cnt["qs1"])
            p.dma("gpsimd", k.DOS.ap(), k.OS.ap(), "st")
            p.dma("gpsimd", k.DQS.ap(), k.QS.ap(), "st")
            p.dma("gpsimd", k.DKS.ap(), k.KS.ap(), "st")
            p.dma("gpsimd", k.DVS.ap(), k.VS.ap(), "st")
            p.dma("gpsimd", k.DAS.ap(), k.AS.ap(), "st")
            p.wait("gpsimd", "st", p.cnt["st"])
        p.wait("gpsimd", "xs", p.cnt["xs"])

        with nc.Block() as block:
            @block.sync
            def _(sync):
                for fn in p.q["sync"]:
                    fn(sync)

            @block.gpsimd
            def _(gpsimd):
                for fn in p.q["gpsimd"]:
                    fn(gpsimd)

            @block.tensor
            def _(tensor):
                for fn in p.q["tensor"]:
                    fn(tensor)

            @block.vector
            def _(vector):
                for fn in p.q["vector"]:
                    fn(vector)

            @block.scalar
            def _(scalar):
                for fn in p.q["scalar"]:
                    fn(scalar)
    return nc


def kernel(**inputs):
    x = np.asarray(inputs["L0_attn_x"])
    nc = build(stage="full")
    in_maps = []
    for c in range(NCORES):
        shard = np.ascontiguousarray(
            x[SEQ * c: SEQ * (c + 1)].transpose(0, 2, 1))
        cst = np.zeros((128, 2), dtype=np.float32)
        cst[:, 0] = EPS
        cst[:, 1] = SCALE
        m = {"x": shard, "cst": cst}
        for name in inputs:
            if name != "L0_attn_x":
                m[name] = np.ascontiguousarray(np.asarray(inputs[name]))
        in_maps.append(m)
    res = run_bass_kernel_spmd(nc, in_maps, core_ids=list(range(NCORES)))
    outs = [np.asarray(r["out"]).transpose(0, 2, 1) for r in res.results]
    return np.concatenate(outs, axis=0).astype(np.float32)
